# version 03
# baseline (speedup 1.0000x reference)
import os
import numpy as np
from contextlib import ExitStack
import concourse.bass as bass
import concourse.mybir as mybir
from concourse.bass_utils import run_bass_kernel_spmd

F32 = mybir.dt.float32
AF = mybir.ActivationFunctionType
ALU = mybir.AluOpType
AX = mybir.AxisListType

D = 1024
H = 16
HD = 64
CH = 128
ES = 256
NE = 16
NS = 16
HF = 256
BC = 2
TC = 512
NEG_BIG = -30000.0


def build(T):
    NT = T // TC
    nc = bass.Bass("TRN2", target_bir_lowering=False, detect_race_conditions=False)

    x_d = nc.dram_tensor("x", [BC, T, D], F32, kind="ExternalInput").ap()
    wwT_d = nc.dram_tensor("wwT", [128, 8, D], F32, kind="ExternalInput").ap()
    wgbaT_d = nc.dram_tensor("wgbaT", [128, 8, 48], F32, kind="ExternalInput").ap()
    woT_d = nc.dram_tensor("woT", [D, D], F32, kind="ExternalInput").ap()
    alog_d = nc.dram_tensor("alog_r", [128, H], F32, kind="ExternalInput").ap()
    dtb_d = nc.dram_tensor("dtb_r", [128, H], F32, kind="ExternalInput").ap()
    gam_d = nc.dram_tensor("gam_r", [128, D], F32, kind="ExternalInput").ap()
    bet_d = nc.dram_tensor("bet_r", [128, D], F32, kind="ExternalInput").ap()
    mu_d = nc.dram_tensor("mu2", [128, 8, ES], F32, kind="ExternalInput").ap()
    scl_d = nc.dram_tensor("scl_r", [128, 1], F32, kind="ExternalInput").ap()
    w1_d = nc.dram_tensor("w1", [NE, 128, 8, HF], F32, kind="ExternalInput").ap()
    w2_d = nc.dram_tensor("w2", [NE, 128, 2, D], F32, kind="ExternalInput").ap()
    b1_d = nc.dram_tensor("b1r", [NE, 32, HF], F32, kind="ExternalInput").ap()
    b2_d = nc.dram_tensor("b2r", [NE, 32, D], F32, kind="ExternalInput").ap()
    out_d = nc.dram_tensor("out", [BC, T, D], F32, kind="ExternalOutput").ap()
    x1_d = nc.dram_tensor("x1sp", [BC, T, D], F32, kind="ExternalOutput" if os.environ.get("KDBG") else "Internal").ap()
    lg_d = nc.dram_tensor("lgsp", [BC, T, ES], F32, kind="ExternalOutput" if os.environ.get("KDBG") else "Internal").ap()
    dbg_d = nc.dram_tensor("dbg", [128, 8, D], F32, kind="ExternalOutput").ap() if os.environ.get("KDBG") else None
    drow_d = nc.dram_tensor("drow", [2, 1, 32 * CH], F32).ap()

    ident_d = nc.inline_tensor(np.eye(128, dtype=np.float32), "ident").ap()
    jj, kk = np.meshgrid(np.arange(128), np.arange(128), indexing="ij")
    trilI_d = nc.inline_tensor((jj <= kk).astype(np.float32), "trilI").ap()
    utS_d = nc.inline_tensor((jj > kk).astype(np.float32), "utS").ap()
    ones_d = nc.inline_tensor(np.ones((128, 128), np.float32), "onesM").ap()
    maskB_d = nc.inline_tensor(np.where(jj < kk, 0.0, NEG_BIG).astype(np.float32), "maskB").ap()

    es = ExitStack()
    sb = lambda name, shape: es.enter_context(nc.sbuf_tensor(name, shape, F32))
    ps = lambda name, shape: es.enter_context(nc.psum_tensor(name, shape, F32))

    wwT_s = sb("wwT_s", [128, 8, D])
    wgba_s = sb("wgba_s", [128, 8, 48])
    mu_s = sb("mu_s", [128, 8, ES])
    gam_s = sb("gam_s", [128, D])
    bet_s = sb("bet_s", [128, D])
    alog_s = sb("alog_s", [128, H])
    dtb_s = sb("dtb_s", [128, H])
    negea_s = sb("negea_s", [128, H])
    scl_s = sb("scl_s", [128, 1])
    ident_s = sb("ident_s", [128, 128])
    trilI_s = sb("trilI_s", [128, 128])
    utS_s = sb("utS_s", [128, 128])
    ones_s = sb("ones_s", [128, 128])
    maskB_s = sb("maskB_s", [128, 128])
    woT_b = sb("woT_b", [128, 2, D])
    w1_s = sb("w1_s", [128, 8, HF])
    w2_s = sb("w2_s", [128, 2, D])
    b1_s = sb("b1_s", [32, HF])
    b2_s = sb("b2_s", [32, D])
    x_s = sb("x_s", [128, D])
    ln0tm_s = sb("ln0tm_s", [128, D])
    ln0fm_s = sb("ln0fm_s", [128, 8, 513])
    s_s = sb("s_s", [128, 4, D])
    vb_s = sb("vb_s", [128, 2, D])
    wkdw_s = sb("wkdw_s", [128, D])
    arena_s = sb("arena_s", [128, 4096])
    intra_s = sb("intra_s", [128, 8, 128])
    ofm_s = sb("ofm_s", [128, 8, 256])
    o3_s = sb("o3_s", [128, D])
    lnt_s = sb("lnt_s", [128, D])
    xn_s = sb("xn_s", [128, D])
    xufm_s = sb("xufm_s", [128, 8, 128])
    decay_s = sb("decay_s", [128, 64])
    dec_s = sb("dec_s", [128, 64])
    dexp_s = sb("dexp_s", [128, 64])
    dw_s = sb("dw_s", [128, 64])
    lam_s = sb("lam_s", [128, 64])
    g2_s = sb("g2_s", [128, 64])
    gate_s = sb("gate_s", [128, 64])
    beta_s = sb("beta_s", [128, 64])
    tmp16_s = sb("tmp16_s", [128, H])
    tmp16b_s = sb("tmp16b_s", [128, H])
    decpm_s = sb("decpm_s", [32, CH])
    S_s = sb("S_s", [128, 8, HD])
    st_s = sb("st_s", [128, 16])
    mz_s = sb("mz_s", [128, 16])
    lg_s = sb("lg_s", [128, 2, 128])
    lgtm_s = sb("lgtm_s", [128, ES])
    pexp_s = sb("pexp_s", [128, 2, 128])
    pexptm_s = sb("pexptm_s", [128, ES])
    acc_s = sb("acc_s", [128, 4, D])
    hg_s = sb("hg_s", [32, HF])
    hT_s = sb("hT_s", [128, 64])
    nrm_s = sb("nrm_s", [1, ES])
    wwT_addr = nc.lookup_mloc(wwT_s).addr
    so_s = nc.alloc_sbuf_tensor_at("so_s", [128, 4, D], F32, offset=wwT_addr)
    sin_s = nc.alloc_sbuf_tensor_at("sin_s", [128, 4096], F32, offset=wwT_addr + 16384)

    pP = ps("pP", [128, 1024])
    pG = ps("pG", [128, 8, 128])
    pA = ps("pA", [128, 1024])
    pB = ps("pB", [128, 1024])

    semT = es.enter_context(nc.semaphore("semT"))
    semV = es.enter_context(nc.semaphore("semV"))
    semA = es.enter_context(nc.semaphore("semA"))
    semDi = es.enter_context(nc.semaphore("semDi"))
    semDo = es.enter_context(nc.semaphore("semDo"))
    semPi = es.enter_context(nc.semaphore("semPi"))
    semPo = es.enter_context(nc.semaphore("semPo"))
    SEMS = {"T": semT, "V": semV, "A": semA, "Di": semDi, "Do": semDo,
            "Pi": semPi, "Po": semPo}

    CNT = {k: 0 for k in SEMS}
    LASTW = {e: {p: 0 for p in SEMS} for e in SEMS}
    PROG = {k: [] for k in SEMS}

    BRIDGE = {"Di": [], "Do": []}
    PROXY = {"Di": "Pi", "Do": "Po"}

    def _bridge(pk, c):
        br = BRIDGE[pk]
        if br and br[-1][0] >= c:
            return (PROXY[pk], br[-1][1])
        cc = CNT[pk]
        CNT[PROXY[pk]] += 1
        pc = CNT[PROXY[pk]]
        sd, sp = SEMS[pk], SEMS[PROXY[pk]]
        PROG[pk].append((lambda h, s1=sd, v=cc, s2=sp: (h.wait_ge(s1, v),
                                                       h.sem_inc(s2, 1))[0],
                         [], sd, 0))
        LASTW[pk][pk] = cc
        br.append((cc, pc))
        return (PROXY[pk], pc)

    STRICT = os.environ.get("KSTRICT", "1") == "1"

    def emit(eng, fn, need=(), inc=False):
        waits = []
        need = list(need)
        if STRICT and fn is not None:
            if eng == "T":
                need += [("V", CNT["V"]), ("A", CNT["A"])]
            elif eng in ("V", "A"):
                need += [("T", CNT["T"])]
                need += [("V", CNT["V"])] if eng == "A" else [("A", CNT["A"])]
        for p in need:
            if isinstance(p, tuple):
                pk, c = p
            else:
                pk, c = p, CNT[p]
            if eng in ("T", "V", "A") and pk in ("Di", "Do"):
                pk, c = _bridge(pk, c)
            if c > LASTW[eng][pk]:
                waits.append((SEMS[pk], c))
                LASTW[eng][pk] = c
        amt = 0
        if inc:
            amt = 16 if eng in ("Di", "Do") else 1
            CNT[eng] += amt
            LASTW[eng][eng] = CNT[eng]
        PROG[eng].append((fn, waits, SEMS[eng], amt))

    def cur(p):
        return (p, CNT[p])

    def dma(eng, out, in_, need=()):
        emit(eng, lambda h, o=out, i=in_: h.dma_start(out=o, in_=i), need=need, inc=True)

    def tmm(out, lhsT, rhs, start, stop, need=(), inc=False):
        emit("T", lambda h, o=out.opt(), l=lhsT.opt(), r=rhs.opt(), s0=start, s1=stop:
             h.matmul(o, l, r, start=s0, stop=s1), need=need, inc=inc)

    def ttr(out, in_, need=(), inc=False, idn=None):
        idn = ident_s.ap() if idn is None else idn
        emit("T", lambda h, o=out, i=in_, d=idn: h.transpose(o, i, d), need=need, inc=inc)

    def vop(name, *args, need=(), inc=True):
        emit("V", lambda h, n=name, a=args: getattr(h, n)(*a), need=need, inc=inc)

    def aact(out, in_, func, bias=0.0, scale=1.0, accum=None, need=(), inc=True):
        emit("A", lambda h, o=out, i=in_, f=func, b=bias, s=scale, ac=accum:
             h.activation(out=o, in_=i, func=f, bias=b, scale=s, accum_out=ac),
             need=need, inc=inc)

    AP = lambda t: t.ap()

    dma("Di", AP(wwT_s), wwT_d)
    dma("Di", AP(wgba_s), wgbaT_d)
    dma("Di", AP(mu_s), mu_d)
    dma("Di", AP(gam_s), gam_d)
    dma("Di", AP(bet_s), bet_d)
    dma("Di", AP(alog_s), alog_d)
    dma("Di", AP(dtb_s), dtb_d)
    dma("Di", AP(scl_s), scl_d)
    dma("Di", AP(ident_s), ident_d)
    dma("Di", AP(trilI_s), trilI_d)
    dma("Di", AP(utS_s), utS_d)
    dma("Di", AP(ones_s), ones_d)
    dma("Di", AP(maskB_s), maskB_d)
    aact(AP(tmp16_s), AP(alog_s), AF.Exp, need=[cur("Di")])
    vop("tensor_scalar_mul", AP(negea_s), AP(tmp16_s), -1.0, need=["A"])
    for kc in range(8):
        aact(AP(wkdw_s)[:, 0:ES], AP(mu_s)[:, kc, :], AF.Square, need=[cur("Di"), "T"])
        tmm(pP[0:1, 0:ES], AP(ones_s)[:, 0:1], AP(wkdw_s)[:, 0:ES],
            start=(kc == 0), stop=(kc == 7), need=["A"], inc=True)
    aact(AP(nrm_s), pP[0:1, 0:ES], AF.Ln, need=["T"])
    aact(AP(nrm_s), AP(nrm_s), AF.Exp, scale=-0.5)
    vop("tensor_scalar_min", AP(nrm_s), AP(nrm_s), 1e12, need=["A"])
    tmm(pP[:, 256:512], AP(ones_s)[0:1, :], AP(nrm_s), start=True, stop=True,
        need=["V"], inc=True)
    for kc in range(8):
        vop("tensor_mul", AP(mu_s)[:, kc, :], AP(mu_s)[:, kc, :], pP[:, 256:512],
            need=["T"])

    def phaseA(b, t, g):
        r0 = t * TC + g * 128
        dma("Di", AP(x_s), x_d[b, r0:r0 + 128, :], need=["V", "A"])
        st = AP(st_s)
        vop("reduce_sum", st[:, 0:1], AP(x_s), AX.X, need=[cur("Di")])
        aact(AP(wkdw_s), AP(x_s), AF.Square, need=[cur("Di"), "V", "T"])
        vop("reduce_sum", st[:, 1:2], AP(wkdw_s), AX.X, need=["A"])
        vop("tensor_scalar_mul", st[:, 2:3], st[:, 0:1], 1.0 / D)
        vop("tensor_scalar_mul", st[:, 3:4], st[:, 1:2], 1.0 / D, need=["A"])
        vop("tensor_mul", st[:, 8:9], st[:, 2:3], st[:, 2:3])
        vop("tensor_sub", st[:, 3:4], st[:, 3:4], st[:, 8:9])
        vop("tensor_scalar_add", st[:, 3:4], st[:, 3:4], 1e-5)
        aact(st[:, 9:10], st[:, 3:4], AF.Ln, need=["V"])
        aact(st[:, 4:5], st[:, 9:10], AF.Exp, scale=-0.5)
        vop("tensor_mul", st[:, 13:14], st[:, 2:3], st[:, 4:5], need=["A"])
        aact(st[:, 5:6], st[:, 4:5], AF.Identity, bias=1.0, need=["V"])
        aact(st[:, 6:7], st[:, 13:14], AF.Identity)
        aact(st[:, 7:8], st[:, 13:14], AF.Identity, scale=-1.0)
        vop("tensor_scalar", AP(s_s)[:, g, :], AP(x_s), st[:, 5:6], st[:, 6:7],
            ALU.mult, ALU.subtract, need=["A"])
        aact(AP(ln0tm_s), AP(x_s), AF.Identity, bias=st[:, 7:8], scale=st[:, 4:5],
             need=["V"])
        for j in range(8):
            ttr(pG[:, j, :], AP(ln0tm_s)[:, j * 128:(j + 1) * 128],
                need=(["A", "V"] if j == 0 else []), inc=(j == 7))
        vop("tensor_copy", AP(ln0fm_s)[:, :, 1 + g * 128: 1 + g * 128 + 128],
            pG[:, :, :], need=["T"])
        for kc in range(8):
            for n in range(2):
                tmm(pA[:, n * 512:(n + 1) * 512],
                    AP(ln0fm_s)[:, kc, 1 + g * 128: 1 + g * 128 + 128],
                    AP(wwT_s)[:, kc, n * 512:(n + 1) * 512],
                    start=(kc == 0), stop=(kc == 7),
                    need=(["V"] if kc == 0 else []), inc=(kc == 7 and n == 1))
        for kc in range(8):
            tmm(pP[:, 512:560], AP(ln0fm_s)[:, kc, 1 + g * 128: 1 + g * 128 + 128],
                AP(wgba_s)[:, kc, :], start=(kc == 0), stop=(kc == 7),
                need=(["V"] if kc == 0 else []), inc=(kc == 7))
        gsl = slice(g * 16, (g + 1) * 16)
        aact(AP(gate_s)[:, gsl], pP[:, 512:528], AF.Sigmoid, need=["T"])
        aact(AP(beta_s)[:, gsl], pP[:, 528:544], AF.Sigmoid)
        vop("tensor_add", AP(tmp16_s), pP[:, 544:560], AP(dtb_s), need=["T"])
        aact(AP(tmp16b_s), AP(tmp16_s), AF.Exp, need=["V"])
        aact(AP(tmp16_s), AP(tmp16b_s), AF.Ln, bias=1.0)
        vop("tensor_mul", AP(decay_s)[:, gsl], AP(tmp16_s), AP(negea_s), need=["A"])
        bAP = AP(beta_s)[:, gsl][:, :, None].broadcast_to([128, 16, 64])
        vop("tensor_mul", AP(vb_s)[:, g % 2, :].rearrange("p (h e) -> p h e", h=16),
            pA[:, :].rearrange("p (h e) -> p h e", h=16), bAP, need=["T", "A"])
        tmm(pP[:, 560:576], AP(trilI_s), AP(decay_s)[:, gsl], True, True,
            need=["V"], inc=False)
        tmm(pP[:, 576:592], AP(ones_s), AP(decay_s)[:, gsl], True, True)
        tmm(pP[:, 592:608], AP(utS_s), AP(decay_s)[:, gsl], True, True, inc=True)
        vop("tensor_copy", AP(dec_s)[:, gsl], pP[:, 560:576], need=["T"])
        aact(AP(dexp_s)[:, gsl], pP[:, 560:576], AF.Exp, need=["T"])
        aact(AP(lam_s)[:, gsl], pP[:, 576:592], AF.Exp)
        aact(AP(dw_s)[:, gsl], pP[:, 592:608], AF.Exp)
        vop("tensor_mul", AP(g2_s)[:, gsl], AP(gate_s)[:, gsl], AP(dexp_s)[:, gsl],
            need=["A"])

    def rep_phase(b, t, hh):
        rk = os.environ.get("KREP", "12345")
        csl = slice(hh * 32, hh * 32 + 32)
        if "1" in rk:
            ttr(pP[0:32, 608:736], AP(dec_s)[:, csl], need=["V", "A"], inc=True)
            aact(AP(decpm_s), pP[0:32, 608:736], AF.Identity, need=["T"])
        if "2" in rk:
            dma("Do", drow_d[hh], AP(decpm_s), need=["A"])
        if "3" in rk:
            dma("Do", AP(arena_s),
                drow_d[hh].broadcast_to([128, 32 * CH]),
                need=[cur("Do"), "V", "A"])
        if "4" in rk:
            a3 = AP(arena_s).rearrange("p (i c) -> p i c", i=32)
            mAP = AP(maskB_s)[:, None, :].broadcast_to([128, 32, 128])
            vop("tensor_add", a3, a3, mAP, need=[cur("Do")])
            dAP = AP(dec_s)[:, csl][:, :, None].broadcast_to([128, 32, 128])
            vop("tensor_sub", a3, a3, dAP)
        if "5" in rk:
            aact(AP(arena_s), AP(arena_s), AF.Exp, need=["V"])

    def scan_chunk(b, t, np_):
        sk = os.environ.get("KSCAN", "giaewkdvtfu")
        li = np_ % 2
        nsl = slice(np_ * 16, (np_ + 1) * 16)
        for half in range(2):
            if "g" not in sk:
                break
            hord = [half * 8 + q for q in (0, 2, 4, 6, 1, 3, 5, 7)]
            for idx, h in enumerate(hord):
                r = 64 * (h % 2)
                j = h // 2
                tmm(pG[:, h % 8, :],
                    AP(ln0fm_s)[r:r + 64, j, np_ * 128: np_ * 128 + 128],
                    AP(ln0fm_s)[r:r + 64, j, 1 + np_ * 128: 1 + np_ * 128 + 128],
                    True, True, need=(["V"] if idx == 0 else []),
                    inc=(idx == 7))
            lsl = AP(arena_s).rearrange("p (i c) -> p i c", i=32)[
                :, li * 16 + half * 8: li * 16 + half * 8 + 8, :]
            if "i" not in sk:
                continue
            vop("tensor_mul", AP(intra_s), pG[:, :, :], lsl, need=["T", "A"])
            if "a" not in sk:
                continue
            for h in range(half * 8, half * 8 + 8):
                tmm(pB[:, h * 64:(h + 1) * 64], AP(intra_s)[:, h % 8, :],
                    AP(vb_s)[:, li, h * 64:(h + 1) * 64], True, True,
                    need=(["V"] if h % 8 == 0 else []), inc=(h % 8 == 7))
        if "e" in sk:
            for idx, h in enumerate(list(range(0, 16, 2)) + list(range(1, 16, 2))):
                r = 64 * (h % 2)
                j = h // 2
                tmm(pA[:, h * 64:(h + 1) * 64],
                    AP(ln0fm_s)[r:r + 64, j, 1 + np_ * 128: 1 + np_ * 128 + 128],
                    AP(S_s)[r:r + 64, j, :], True, True,
                    need=(["V"] if idx == 0 else []), inc=(idx == 15))
        if "w" not in sk:
            return
        for j in range(8):
            ttr(pG[:, j, :], AP(ln0fm_s)[:, j, np_ * 128: np_ * 128 + 128],
                need=(["V"] if j == 0 else []), inc=(j == 7))
        dwAP = AP(dw_s)[:, nsl].rearrange("p (j par) -> p j par", par=2)[
            :, :, :, None].broadcast_to([128, 8, 2, 64])
        if "k" not in sk:
            return
        vop("tensor_mul",
            AP(wkdw_s).rearrange("p (j par dd) -> p j par dd", j=8, par=2),
            pG[:, :, :].rearrange("p j (par dd) -> p j par dd", par=2),
            dwAP, need=["T", "A"])
        if "d" not in sk:
            return
        for idx, h in enumerate(list(range(0, 16, 2)) + list(range(1, 16, 2))):
            r = 64 * (h % 2)
            j = h // 2
            tmm(pP[r:r + 64, j * 64:(j + 1) * 64],
                AP(wkdw_s)[:, j * 128 + (h % 2) * 64: j * 128 + (h % 2) * 64 + 64],
                AP(vb_s)[:, li, h * 64:(h + 1) * 64], True, True,
                need=(["V"] if idx == 0 else []), inc=(idx == 15))
        if "v" not in sk:
            return
        g2AP = AP(g2_s)[:, nsl][:, :, None].broadcast_to([128, 16, 64])
        gAP = AP(gate_s)[:, nsl][:, :, None].broadcast_to([128, 16, 64])
        o3v = lambda s: AP(s).rearrange("p (h e) -> p h e", h=16)
        vop("tensor_mul", o3v(x_s), pA[:, :].rearrange("p (h e) -> p h e", h=16),
            g2AP, need=["T"])
        vop("tensor_mul", o3v(ln0tm_s), pB[:, :].rearrange("p (h e) -> p h e", h=16),
            gAP)
        vop("tensor_add", AP(o3_s), AP(x_s), AP(ln0tm_s))
        if "t" not in sk:
            return
        for j in range(8):
            ttr(pG[:, j, :], AP(o3_s)[:, j * 128:(j + 1) * 128],
                need=(["V"] if j == 0 else []), inc=(j == 7))
        if "f" not in sk:
            return
        vop("tensor_copy", AP(ofm_s)[:, :, li * 128: li * 128 + 128], pG[:, :, :],
            need=["T"])
        if "u" not in sk:
            return
        for r in range(2):
            lAP = AP(lam_s)[64 * r:64 * r + 64,
                            np_ * 16 + r: np_ * 16 + 16: 2][:, :, None] \
                .broadcast_to([64, 8, 64])
            vop("tensor_mul", AP(S_s)[64 * r:64 * r + 64, :, :],
                AP(S_s)[64 * r:64 * r + 64, :, :], lAP,
                need=(["T"] if r == 0 else []))
        vop("tensor_add", AP(S_s)[:, :, :],
            AP(S_s)[:, :, :], pP[:, 0:512].rearrange("p (j e) -> p j e", j=8))

    def wout_half(b, t, hh):
        pmap = [pA, pB]
        tdone = [CNT["T"]] * 2
        for kc in range(8):
            dma("Di", AP(woT_b)[:, kc % 2, :], woT_d[kc * 128:(kc + 1) * 128, :],
                need=[("T", tdone[kc % 2])])
            for gl in range(2):
                for n in range(2):
                    tmm(pmap[gl][:, n * 512:(n + 1) * 512],
                        AP(ofm_s)[:, kc, gl * 128: gl * 128 + 128],
                        AP(woT_b)[:, kc % 2, n * 512:(n + 1) * 512],
                        start=(kc == 0), stop=(kc == 7),
                        need=[cur("Di")] + (["V"] if kc == 0 else []),
                        inc=(gl == 1 and n == 1))
            tdone[kc % 2] = CNT["T"]
        for gl in range(2):
            g = 2 * hh + gl
            vop("tensor_add", AP(s_s)[:, g, :], AP(s_s)[:, g, :], pmap[gl][:, :],
                need=["T"])
            r0 = t * TC + g * 128
            dma("Do", x1_d[b, r0:r0 + 128, :], AP(s_s)[:, g, :], need=["V"])

    def ffn_g(b, t, g):
        fk = os.environ.get("KFFN", "123456")
        x1 = AP(s_s)[:, g, :]
        st = AP(st_s)
        r0 = t * TC + g * 128

        def ln_stats(src, ci, need0):
            vop("reduce_sum", st[:, 0:1], src, AX.X, need=need0)
            aact(AP(wkdw_s), src, AF.Square, need=need0 + ["V"])
            vop("reduce_sum", st[:, 1:2], AP(wkdw_s), AX.X, need=["A"])
            vop("tensor_scalar_mul", st[:, 2:3], st[:, 0:1], 1.0 / D)
            vop("tensor_scalar_mul", st[:, 3:4], st[:, 1:2], 1.0 / D, need=["A"])
            vop("tensor_mul", st[:, 8:9], st[:, 2:3], st[:, 2:3])
            vop("tensor_sub", st[:, 3:4], st[:, 3:4], st[:, 8:9])
            vop("tensor_scalar_add", st[:, 3:4], st[:, 3:4], 1e-5)
            aact(st[:, 9:10], st[:, 3:4], AF.Ln, need=["V"])
            aact(st[:, 4:5], st[:, 9:10], AF.Exp, scale=-0.5)
            vop("tensor_mul", st[:, 13:14], st[:, 2:3], st[:, 4:5], need=["A"])
            aact(st[:, 7:8], st[:, 13:14], AF.Identity, scale=-1.0, need=["V"])

        if "1" not in fk:
            return
        ln_stats(x1, 0, ["V", "A", "T"])
        aact(AP(lnt_s), x1, AF.Identity, bias=st[:, 7:8], scale=st[:, 4:5],
             need=["V"])
        ln_stats(AP(lnt_s), 0, ["A"])
        aact(AP(o3_s), AP(lnt_s), AF.Identity, bias=st[:, 7:8], scale=st[:, 4:5],
             need=["V"])
        vop("tensor_mul", AP(xn_s), AP(o3_s), AP(gam_s), need=["A"])
        vop("tensor_add", AP(xn_s), AP(xn_s), AP(bet_s))
        aact(AP(wkdw_s), AP(xn_s), AF.Square, need=["V"])
        vop("reduce_sum", st[:, 1:2], AP(wkdw_s), AX.X, need=["A"])
        aact(st[:, 9:10], st[:, 1:2], AF.Ln, need=["V"])
        aact(st[:, 10:11], st[:, 9:10], AF.Exp, scale=-0.5)
        vop("tensor_scalar_min", st[:, 10:11], st[:, 10:11], 1e12, need=["A"])
        vop("tensor_mul", st[:, 11:12], st[:, 10:11], AP(scl_s))
        aact(AP(lnt_s), AP(xn_s), AF.Identity, bias=0.0, scale=st[:, 11:12],
             need=["V"])
        if "2" not in fk:
            return
        for j in range(8):
            ttr(pG[:, j, :], AP(lnt_s)[:, j * 128:(j + 1) * 128],
                need=(["A", "V"] if j == 0 else []), inc=(j == 7))
        vop("tensor_copy", AP(xufm_s)[:, :, :], pG[:, :, :], need=["T"])
        if "3" not in fk:
            return
        for esc in range(2):
            for kc in range(8):
                tmm(pP[:, esc * 128:(esc + 1) * 128],
                    AP(mu_s)[:, kc, esc * 128:(esc + 1) * 128],
                    AP(xufm_s)[:, kc, :], start=(kc == 0), stop=(kc == 7),
                    need=(["V"] if kc == 0 else []),
                    inc=(kc == 7 and esc == 1))
        vop("tensor_copy", AP(lg_s)[:, :, :],
            pP[:, 0:256].rearrange("p (i c) -> p i c", i=2), need=["T"])
        if "4" not in fk:
            return
        f4 = os.environ.get("KFF4", "abcdef")
        mz = AP(mz_s)
        bm = slice(b * 2, b * 2 + 2)
        if "a" in f4:
            for esc in range(2):
                vop("reduce_max", st[:, 12 + esc:13 + esc], AP(lg_s)[:, esc, :], AX.X)
        if "b" in f4:
            vop("tensor_max", st[:, 12:14], st[:, 12:14], mz[:, bm])
            vop("tensor_sub", st[:, 14:16], mz[:, bm], st[:, 12:14])
        if "c" in f4:
            aact(mz[:, 8 + b * 2: 8 + b * 2 + 2], st[:, 14:16], AF.Exp, need=["V"])
        if "d" in f4:
            vop("tensor_copy", mz[:, bm], st[:, 12:14])
            vop("tensor_scalar_mul", mz[:, 12 + b * 2: 12 + b * 2 + 2],
                st[:, 12:14], -1.0)
        if "e" in f4:
            for esc in range(2):
                aact(AP(pexp_s)[:, esc, :], AP(lg_s)[:, esc, :], AF.Exp,
                     bias=mz[:, 12 + b * 2 + esc: 12 + b * 2 + esc + 1],
                     need=["V"])
            for esc in range(2):
                vop("reduce_sum", st[:, 5 + esc: 6 + esc], AP(pexp_s)[:, esc, :],
                    AX.X, need=["A"])
        if "f" in f4:
            for esc in range(2):
                vop("scalar_tensor_tensor", mz[:, 4 + b * 2 + esc: 4 + b * 2 + esc + 1],
                    mz[:, 4 + b * 2 + esc: 4 + b * 2 + esc + 1],
                    mz[:, 8 + b * 2 + esc: 8 + b * 2 + esc + 1],
                    st[:, 5 + esc: 6 + esc], ALU.mult, ALU.add, need=["A"])
        if "5" not in fk:
            return
        for esc in range(2):
            ttr(pG[:, esc, :], AP(pexp_s)[:, esc, :],
                need=(["A", "V"] if esc == 0 else []))
            ttr(pG[:, 2 + esc, :], AP(lg_s)[:, esc, :], inc=(esc == 1))
        vop("tensor_copy", AP(pexptm_s).rearrange("p (i c) -> p i c", i=2),
            pG[:, 0:2, :], need=["T"])
        vop("tensor_copy", AP(lgtm_s).rearrange("p (i c) -> p i c", i=2),
            pG[:, 2:4, :])
        dma("Do", lg_d[b, r0:r0 + 128, :], AP(lgtm_s), need=["V"])
        if "6" not in fk:
            return
        for esc in range(2):
            for n in range(2):
                tmm((pA if esc == 0 else pB)[:, n * 512:(n + 1) * 512],
                    AP(pexptm_s)[:, esc * 128:(esc + 1) * 128],
                    AP(xn_s)[:, n * 512:(n + 1) * 512], True, True,
                    need=(["V"] if esc == 0 and n == 0 else []),
                    inc=(esc == 1 and n == 1))
        for esc in range(2):
            vop("scalar_tensor_tensor", AP(acc_s)[:, b * 2 + esc, :],
                AP(acc_s)[:, b * 2 + esc, :],
                mz[:, 8 + b * 2 + esc: 8 + b * 2 + esc + 1],
                (pA if esc == 0 else pB)[:, :], ALU.mult, ALU.add, need=["T"])

    def slot_phase():
        for b in range(2):
            for esc in range(2):
                sec = b * 2 + esc
                aact(AP(st_s)[:, 1:2], AP(mz_s)[:, 4 + sec: 5 + sec], AF.Ln,
                     need=["T", "V"])
                aact(AP(st_s)[:, 0:1], AP(st_s)[:, 1:2], AF.Exp, scale=-1.0)
                vop("tensor_scalar_mul", AP(acc_s)[:, sec, :],
                    AP(acc_s)[:, sec, :], AP(st_s)[:, 0:1], need=["T", "A"])
                for kc in range(8):
                    ttr(pG[:, kc, :], AP(acc_s)[:, sec, kc * 128:(kc + 1) * 128],
                        need=(["V"] if kc == 0 else []), inc=(kc == 7))
                oAP = bass.AP(sin_s, esc * 256 + b * 16,
                              [[4096, 128], [512, 8], [32, 8], [1, 16]])
                iAP = bass.AP(pG, 0, [[1024, 128], [128, 8], [16, 8], [1, 16]])
                vop("tensor_copy", oAP, iAP, need=["T"])
        dma("Di", AP(w1_s), w1_d[0], need=["T"])
        dma("Di", AP(b1_s), b1_d[0])
        dma("Di", AP(b2_s), b2_d[0])
        w1c, b1c, b2c, w2c = CNT["Di"], CNT["Di"], CNT["Di"], 0
        for e in range(NE):
            for kc in range(8):
                lh = bass.AP(sin_s, kc * 512 + (e // 8) * 256 + (e % 8) * 32,
                             [[4096, 128], [1, 32]])
                tmm(pP[0:32, 0:HF], lh, AP(w1_s)[:, kc, :],
                    start=(kc == 0), stop=(kc == 7),
                    need=[("Di", w1c)] + (["V"] if kc == 0 else []),
                    inc=(kc == 7))
            dma("Di", AP(w2_s), w2_d[e], need=["T"])
            w2c = CNT["Di"]
            vop("tensor_add", AP(hg_s), pP[0:32, 0:HF], AP(b1_s),
                need=["T", ("Di", b1c)])
            aact(AP(hg_s), AP(hg_s), AF.Gelu, need=["V"])
            for hc in range(2):
                ttr(pG[0:128, hc, 0:32], AP(hg_s)[:, hc * 128:(hc + 1) * 128],
                    need=(["A"] if hc == 0 else []), idn=AP(ident_s)[0:32, 0:32],
                    inc=(hc == 1))
            vop("tensor_copy", AP(hT_s).rearrange("p (hc q) -> p hc q", hc=2),
                pG[:, 0:2, 0:32], need=["T"])
            for hc in range(2):
                for n in range(2):
                    tmm(pA[0:32, n * 512:(n + 1) * 512], AP(hT_s)[:, hc * 32:hc * 32 + 32],
                        AP(w2_s)[:, hc, n * 512:(n + 1) * 512],
                        start=(hc == 0), stop=(hc == 1),
                        need=[("Di", w2c)] + (["V"] if hc == 0 and n == 0 else []),
                        inc=(hc == 1 and n == 1))
            if e + 1 < NE:
                dma("Di", AP(w1_s), w1_d[e + 1], need=["T"])
                w1c = CNT["Di"]
            vop("tensor_add", AP(wkdw_s)[0:32, :], pA[0:32, :], AP(b2_s),
                need=["T", "Do", ("Di", b2c)])
            if e + 1 < NE:
                dma("Di", AP(b1_s), b1_d[e + 1], need=["V"])
                b1c = CNT["Di"]
                dma("Di", AP(b2_s), b2_d[e + 1], need=["V"])
                b2c = CNT["Di"]
            for b in range(2):
                dma("Do", AP(so_s)[(e % 8) * 16:(e % 8) * 16 + 16, b * 2 + e // 8, :],
                    AP(wkdw_s)[b * 16: b * 16 + 16, :], need=["V"])

    def pass2_g(b, t, g):
        r0 = t * TC + g * 128
        dma("Di", AP(x_s), x1_d[b, r0:r0 + 128, :], need=["V", "Do"])
        dma("Di", AP(lgtm_s), lg_d[b, r0:r0 + 128, :], need=["V", "A", "T"])
        st = AP(st_s)
        vop("reduce_max", st[:, 0:1], AP(lgtm_s), AX.X, need=[cur("Di")])
        vop("tensor_scalar_mul", st[:, 1:2], st[:, 0:1], -1.0)
        aact(AP(pexptm_s), AP(lgtm_s), AF.Exp, bias=st[:, 1:2], need=["V"])
        vop("reduce_sum", st[:, 2:3], AP(pexptm_s), AX.X, need=["A"])
        aact(st[:, 8:9], st[:, 2:3], AF.Ln, need=["V"])
        aact(st[:, 3:4], st[:, 8:9], AF.Exp, scale=-1.0)
        for esc in range(2):
            ttr(pG[:, esc, :], AP(pexptm_s)[:, esc * 128:(esc + 1) * 128],
                need=(["A", "V"] if esc == 0 else []), inc=(esc == 1))
        vop("tensor_copy", AP(lg_s)[:, :, :], pG[:, 0:2, :], need=["T"])
        for esc in range(2):
            for n in range(2):
                tmm(pP[:, n * 512:(n + 1) * 512], AP(lg_s)[:, esc, :],
                    AP(so_s)[:, b * 2 + esc, n * 512:(n + 1) * 512],
                    start=(esc == 0), stop=(esc == 1),
                    need=(["V", "Do"] if esc == 0 and n == 0 else []),
                    inc=(esc == 1 and n == 1))
        vop("scalar_tensor_tensor", AP(xn_s), pP[:, :], st[:, 3:4], AP(x_s),
            ALU.mult, ALU.add, need=["T", "A"])
        dma("Do", out_d[b, r0:r0 + 128, :], AP(xn_s), need=["V"])

    _ks = os.environ.get("KSTAGES", "a,r,s,w,f,sl,p2").split(",")
    for b in range(BC):
        vop("memset", AP(S_s)[:, :, :], 0.0, need=["T"])
        vop("memset", AP(acc_s)[:, b * 2:b * 2 + 2, :], 0.0)
        vop("memset", AP(mz_s)[:, b * 2:b * 2 + 2], -100.0)
        vop("memset", AP(mz_s)[:, 4 + b * 2:4 + b * 2 + 2], 0.0)
        vop("memset", AP(ln0fm_s)[:, :, 0:1], 0.0)
        for t in range(NT):
            for hh in range(2):
                if "a" in _ks:
                    phaseA(b, t, 2 * hh)
                    phaseA(b, t, 2 * hh + 1)
                if "r" in _ks:
                    rep_phase(b, t, hh)
                if "s" in _ks:
                    scan_chunk(b, t, 2 * hh)
                    scan_chunk(b, t, 2 * hh + 1)
                if "w" in _ks:
                    wout_half(b, t, hh)
            if "f" in _ks:
                for g in range(4):
                    ffn_g(b, t, g)
            if "a" in _ks:
                vop("tensor_copy", AP(ln0fm_s)[:, :, 0:1],
                    AP(ln0fm_s)[:, :, 512:513], need=["T"])
    if "sl" in _ks:
        slot_phase()
    if "p2" in _ks:
        for b in range(BC):
            for t in range(NT):
                for g in range(4):
                    pass2_g(b, t, g)

    if os.environ.get("KDBG"):
        dma("Do", dbg_d[:, 0:4, :], AP(acc_s), need=["V", "A", "T"])
        if "sl" in _ks:
            dma("Do", dbg_d[:, 4:8, :], AP(so_s), need=["V", "A", "T"])
        dbg2_d = nc.dram_tensor("dbg2", [128, 16, 1024], F32, kind="ExternalOutput").ap()
        dbg3_d = nc.dram_tensor("dbg3", [128, 11, 1024], F32, kind="ExternalOutput").ap()
        dma("Do", dbg2_d[:, 0, 0:64], AP(dec_s))
        dma("Do", dbg2_d[:, 0, 64:128], AP(dexp_s))
        dma("Do", dbg2_d[:, 0, 128:192], AP(dw_s))
        dma("Do", dbg2_d[:, 0, 192:256], AP(lam_s))
        dma("Do", dbg2_d[:, 0, 256:320], AP(g2_s))
        dma("Do", dbg2_d[:, 0, 320:384], AP(gate_s))
        dma("Do", dbg2_d[:, 0, 384:448], AP(beta_s))
        if "r" in _ks:
            dma("Do", dbg2_d[:, 1:5, :], AP(arena_s).rearrange("p (a b) -> p a b", a=4))
        if "s" in _ks:
            dma("Do", dbg2_d[:, 5, :], AP(intra_s).rearrange("p a b -> p (a b)"))
        if "s" in _ks:
            dma("Do", dbg2_d[:, 6:8, :], AP(ofm_s).rearrange("p a b -> p (a b)").rearrange("p (a b) -> p a b", a=2))
            dma("Do", dbg2_d[:, 8, 0:512], AP(S_s).rearrange("p a b -> p (a b)"))
        dma("Do", dbg2_d[:, 9:11, :], AP(vb_s).rearrange("p a b -> p a b"))
        if "s" in _ks:
            dma("Do", dbg2_d[:, 11, :], AP(wkdw_s))
        dma("Do", dbg2_d[:, 12:16, :], AP(s_s))
        dma("Do", dbg3_d[:, 0, :], AP(x_s))
        dma("Do", dbg3_d[:, 1, :], AP(ln0tm_s))
        dma("Do", dbg3_d[:, 2:10, :], AP(ln0fm_s)[:, :, 0:513], allow_pad=True) if False else None
        dma("Do", dbg3_d[:, 2, 0:16], AP(st_s))
        dma("Do", dbg3_d[:, 3:11, :512], AP(ln0fm_s)[:, :, 1:513])
    emit("Do", None, need=["Do", "Di", "V", "A", "T"])

    def runner(h, items):
        for fn, waits, sem, amt in items:
            for s, c in waits:
                h.wait_ge(s, c)
            if fn is not None:
                ins = fn(h)
                if amt:
                    ins.then_inc(sem, amt)

    with nc.Block() as block:
        @block.tensor
        def _(h):
            runner(h, PROG["T"])

        @block.vector
        def _(h):
            runner(h, PROG["V"])

        @block.scalar
        def _(h):
            runner(h, PROG["A"])

        @block.sync
        def _(h):
            runner(h, PROG["Di"])

        @block.gpsimd
        def _(h):
            runner(h, PROG["Do"])

    es.close()
    return nc


def make_inputs(full, core):
    b0 = core * BC
    def kcmaj(w, nk):
        return np.ascontiguousarray(w.reshape(nk, 128, -1).transpose(1, 0, 2))

    wgba = np.concatenate([full["L0_attn_W_gate"].T, full["L0_attn_W_beta"].T,
                           full["L0_attn_W_alpha"].T], axis=1)
    return {
        "x": np.ascontiguousarray(full["L0_attn_out"][b0:b0 + BC]),
        "wwT": kcmaj(full["L0_attn_W_write"].T, 8),
        "wgbaT": kcmaj(wgba, 8),
        "woT": np.ascontiguousarray(full["L0_attn_W_out"].T),
        "alog_r": np.ascontiguousarray(np.tile(full["L0_attn_A_log"][None, :], (128, 1))),
        "dtb_r": np.ascontiguousarray(np.tile(full["L0_attn_dt_bias"][None, :], (128, 1))),
        "gam_r": np.ascontiguousarray(np.tile(full["L0_ffn_ln_gamma"][None, :], (128, 1))),
        "bet_r": np.ascontiguousarray(np.tile(full["L0_ffn_ln_beta"][None, :], (128, 1))),
        "mu2": kcmaj(full["L0_ffn_mu"].reshape(D, ES), 8),
        "scl_r": np.ascontiguousarray(np.tile(full["L0_ffn_scale"].reshape(1, 1), (128, 1))),
        "w1": np.ascontiguousarray(np.stack([kcmaj(full["L0_ffn_w1"][e], 8) for e in range(NE)])),
        "w2": np.ascontiguousarray(np.stack([kcmaj(full["L0_ffn_w2"][e], 2) for e in range(NE)])),
        "b1r": np.ascontiguousarray(np.tile(full["L0_ffn_b1"][:, None, :], (1, 32, 1))),
        "b2r": np.ascontiguousarray(np.tile(full["L0_ffn_b2"][:, None, :], (1, 32, 1))),
    }


def kernel(**inputs):
    T = inputs["L0_attn_out"].shape[1]
    nc = build(T)
    full = {k: np.asarray(v, dtype=np.float32) for k, v in inputs.items()}
    in_maps = [make_inputs(full, c) for c in range(8)]
    res = run_bass_kernel_spmd(nc, in_maps, core_ids=list(range(8)))
    out = np.concatenate([r["out"] for r in res.results], axis=0)
    return out.astype(np.float32)


# version 04 (final)
# speedup vs baseline: 1.0298x; 1.0298x over previous
import os
import numpy as np
from contextlib import ExitStack
import concourse.bass as bass
import concourse.mybir as mybir
from concourse.bass_utils import run_bass_kernel_spmd

F32 = mybir.dt.float32
AF = mybir.ActivationFunctionType
ALU = mybir.AluOpType
AX = mybir.AxisListType

D = 1024
H = 16
HD = 64
CH = 128
ES = 256
NE = 16
NS = 16
HF = 256
BC = 2
TC = 512
NEG_BIG = -30000.0


def build(T):
    NT = T // TC
    nc = bass.Bass("TRN2", target_bir_lowering=False, detect_race_conditions=False)

    x_d = nc.dram_tensor("x", [BC, T, D], F32, kind="ExternalInput").ap()
    wwT_d = nc.dram_tensor("wwT", [128, 8, D], F32, kind="ExternalInput").ap()
    wgbaT_d = nc.dram_tensor("wgbaT", [128, 8, 48], F32, kind="ExternalInput").ap()
    woT_d = nc.dram_tensor("woT", [D, D], F32, kind="ExternalInput").ap()
    alog_d = nc.dram_tensor("alog_r", [128, H], F32, kind="ExternalInput").ap()
    dtb_d = nc.dram_tensor("dtb_r", [128, H], F32, kind="ExternalInput").ap()
    gam_d = nc.dram_tensor("gam_r", [128, D], F32, kind="ExternalInput").ap()
    bet_d = nc.dram_tensor("bet_r", [128, D], F32, kind="ExternalInput").ap()
    mu_d = nc.dram_tensor("mu2", [128, 8, ES], F32, kind="ExternalInput").ap()
    scl_d = nc.dram_tensor("scl_r", [128, 1], F32, kind="ExternalInput").ap()
    w1_d = nc.dram_tensor("w1", [NE, 128, 8, HF], F32, kind="ExternalInput").ap()
    w2_d = nc.dram_tensor("w2", [NE, 128, 2, D], F32, kind="ExternalInput").ap()
    b1_d = nc.dram_tensor("b1r", [NE, 32, HF], F32, kind="ExternalInput").ap()
    b2_d = nc.dram_tensor("b2r", [NE, 32, D], F32, kind="ExternalInput").ap()
    out_d = nc.dram_tensor("out", [BC, T, D], F32, kind="ExternalOutput").ap()
    x1_d = nc.dram_tensor("x1sp", [BC, T, D], F32, kind="ExternalOutput" if os.environ.get("KDBG") else "Internal").ap()
    lg_d = nc.dram_tensor("lgsp", [BC, T, ES], F32, kind="ExternalOutput" if os.environ.get("KDBG") else "Internal").ap()
    dbg_d = nc.dram_tensor("dbg", [128, 8, D], F32, kind="ExternalOutput").ap() if os.environ.get("KDBG") else None
    drow_d = nc.dram_tensor("drow", [2, 1, 32 * CH], F32).ap()

    ident_d = nc.inline_tensor(np.eye(128, dtype=np.float32), "ident").ap()
    jj, kk = np.meshgrid(np.arange(128), np.arange(128), indexing="ij")
    trilI_d = nc.inline_tensor((jj <= kk).astype(np.float32), "trilI").ap()
    utS_d = nc.inline_tensor((jj > kk).astype(np.float32), "utS").ap()
    ones_d = nc.inline_tensor(np.ones((128, 128), np.float32), "onesM").ap()
    maskB_d = nc.inline_tensor(np.where(jj < kk, 0.0, NEG_BIG).astype(np.float32), "maskB").ap()

    es = ExitStack()
    sb = lambda name, shape: es.enter_context(nc.sbuf_tensor(name, shape, F32))
    ps = lambda name, shape: es.enter_context(nc.psum_tensor(name, shape, F32))

    wwT_s = sb("wwT_s", [128, 8, D])
    wgba_s = sb("wgba_s", [128, 8, 48])
    mu_s = sb("mu_s", [128, 8, ES])
    gam_s = sb("gam_s", [128, D])
    bet_s = sb("bet_s", [128, D])
    alog_s = sb("alog_s", [128, H])
    dtb_s = sb("dtb_s", [128, H])
    negea_s = sb("negea_s", [128, H])
    scl_s = sb("scl_s", [128, 1])
    ident_s = sb("ident_s", [128, 128])
    trilI_s = sb("trilI_s", [128, 128])
    utS_s = sb("utS_s", [128, 128])
    ones_s = sb("ones_s", [128, 128])
    maskB_s = sb("maskB_s", [128, 128])
    woT_b = sb("woT_b", [128, 2, D])
    w1_s = sb("w1_s", [128, 8, HF])
    w2_s = sb("w2_s", [128, 2, D])
    b1_s = sb("b1_s", [32, HF])
    b2_s = sb("b2_s", [32, D])
    x_s = sb("x_s", [128, D])
    ln0tm_s = sb("ln0tm_s", [128, D])
    ln0fm_s = sb("ln0fm_s", [128, 8, 513])
    s_s = sb("s_s", [128, 4, D])
    vb_s = sb("vb_s", [128, 2, D])
    wkdw_s = sb("wkdw_s", [128, D])
    arena_s = sb("arena_s", [128, 4096])
    intra_s = sb("intra_s", [128, 8, 128])
    ofm_s = sb("ofm_s", [128, 8, 256])
    o3_s = sb("o3_s", [128, D])
    lnt_s = sb("lnt_s", [128, D])
    xn_s = sb("xn_s", [128, D])
    xufm_s = sb("xufm_s", [128, 8, 128])
    decay_s = sb("decay_s", [128, 64])
    dec_s = sb("dec_s", [128, 64])
    dexp_s = sb("dexp_s", [128, 64])
    dw_s = sb("dw_s", [128, 64])
    lam_s = sb("lam_s", [128, 64])
    g2_s = sb("g2_s", [128, 64])
    gate_s = sb("gate_s", [128, 64])
    beta_s = sb("beta_s", [128, 64])
    tmp16_s = sb("tmp16_s", [128, H])
    tmp16b_s = sb("tmp16b_s", [128, H])
    decpm_s = sb("decpm_s", [32, CH])
    S_s = sb("S_s", [128, 8, HD])
    st_s = sb("st_s", [128, 16])
    mz_s = sb("mz_s", [128, 16])
    lg_s = sb("lg_s", [128, 2, 128])
    lgtm_s = sb("lgtm_s", [128, ES])
    pexp_s = sb("pexp_s", [128, 2, 128])
    pexptm_s = sb("pexptm_s", [128, ES])
    acc_s = sb("acc_s", [128, 4, D])
    hg_s = sb("hg_s", [32, HF])
    hT_s = sb("hT_s", [128, 64])
    nrm_s = sb("nrm_s", [1, ES])
    wwT_addr = nc.lookup_mloc(wwT_s).addr
    so_s = nc.alloc_sbuf_tensor_at("so_s", [128, 4, D], F32, offset=wwT_addr)
    sin_s = nc.alloc_sbuf_tensor_at("sin_s", [128, 4096], F32, offset=wwT_addr + 16384)

    pP = ps("pP", [128, 1024])
    pG = ps("pG", [128, 8, 128])
    pA = ps("pA", [128, 1024])
    pB = ps("pB", [128, 1024])

    semT = es.enter_context(nc.semaphore("semT"))
    semV = es.enter_context(nc.semaphore("semV"))
    semA = es.enter_context(nc.semaphore("semA"))
    semDi = es.enter_context(nc.semaphore("semDi"))
    semDo = es.enter_context(nc.semaphore("semDo"))
    semPi = es.enter_context(nc.semaphore("semPi"))
    semPo = es.enter_context(nc.semaphore("semPo"))
    SEMS = {"T": semT, "V": semV, "A": semA, "Di": semDi, "Do": semDo,
            "Pi": semPi, "Po": semPo}

    CNT = {k: 0 for k in SEMS}
    LASTW = {e: {p: 0 for p in SEMS} for e in SEMS}
    PROG = {k: [] for k in SEMS}

    BRIDGE = {"Di": [], "Do": []}
    PROXY = {"Di": "Pi", "Do": "Po"}

    def _bridge(pk, c):
        br = BRIDGE[pk]
        if br and br[-1][0] >= c:
            return (PROXY[pk], br[-1][1])
        cc = CNT[pk]
        CNT[PROXY[pk]] += 1
        pc = CNT[PROXY[pk]]
        sd, sp = SEMS[pk], SEMS[PROXY[pk]]
        PROG[pk].append((lambda h, s1=sd, v=cc, s2=sp: (h.wait_ge(s1, v),
                                                       h.sem_inc(s2, 1))[0],
                         [], sd, 0))
        LASTW[pk][pk] = cc
        br.append((cc, pc))
        return (PROXY[pk], pc)

    STRICT = os.environ.get("KSTRICT", "0") == "1"

    def emit(eng, fn, need=(), inc=False):
        waits = []
        need = list(need)
        if STRICT and fn is not None:
            if eng == "T":
                need += [("V", CNT["V"]), ("A", CNT["A"])]
            elif eng in ("V", "A"):
                need += [("T", CNT["T"])]
                need += [("V", CNT["V"])] if eng == "A" else [("A", CNT["A"])]
        for p in need:
            if isinstance(p, tuple):
                pk, c = p
            else:
                pk, c = p, CNT[p]
            if eng in ("T", "V", "A") and pk in ("Di", "Do"):
                pk, c = _bridge(pk, c)
            if c > LASTW[eng][pk]:
                waits.append((SEMS[pk], c))
                LASTW[eng][pk] = c
        amt = 0
        if inc:
            amt = 16 if eng in ("Di", "Do") else 1
            CNT[eng] += amt
            LASTW[eng][eng] = CNT[eng]
        PROG[eng].append((fn, waits, SEMS[eng], amt))

    def cur(p):
        return (p, CNT[p])

    def dma(eng, out, in_, need=()):
        emit(eng, lambda h, o=out, i=in_: h.dma_start(out=o, in_=i), need=need, inc=True)

    def tmm(out, lhsT, rhs, start, stop, need=(), inc=False):
        emit("T", lambda h, o=out.opt(), l=lhsT.opt(), r=rhs.opt(), s0=start, s1=stop:
             h.matmul(o, l, r, start=s0, stop=s1), need=need, inc=inc)

    def ttr(out, in_, need=(), inc=False, idn=None):
        idn = ident_s.ap() if idn is None else idn
        emit("T", lambda h, o=out, i=in_, d=idn: h.transpose(o, i, d), need=need, inc=inc)

    def vop(name, *args, need=(), inc=True):
        emit("V", lambda h, n=name, a=args: getattr(h, n)(*a), need=need, inc=inc)

    def aact(out, in_, func, bias=0.0, scale=1.0, accum=None, need=(), inc=True):
        emit("A", lambda h, o=out, i=in_, f=func, b=bias, s=scale, ac=accum:
             h.activation(out=o, in_=i, func=f, bias=b, scale=s, accum_out=ac),
             need=need, inc=inc)

    AP = lambda t: t.ap()

    dma("Di", AP(wwT_s), wwT_d)
    dma("Di", AP(wgba_s), wgbaT_d)
    dma("Di", AP(mu_s), mu_d)
    dma("Di", AP(gam_s), gam_d)
    dma("Di", AP(bet_s), bet_d)
    dma("Di", AP(alog_s), alog_d)
    dma("Di", AP(dtb_s), dtb_d)
    dma("Di", AP(scl_s), scl_d)
    dma("Di", AP(ident_s), ident_d)
    dma("Di", AP(trilI_s), trilI_d)
    dma("Di", AP(utS_s), utS_d)
    dma("Di", AP(ones_s), ones_d)
    dma("Di", AP(maskB_s), maskB_d)
    aact(AP(tmp16_s), AP(alog_s), AF.Exp, need=[cur("Di")])
    vop("tensor_scalar_mul", AP(negea_s), AP(tmp16_s), -1.0, need=["A"])
    for kc in range(8):
        aact(AP(wkdw_s)[:, 0:ES], AP(mu_s)[:, kc, :], AF.Square, need=[cur("Di"), "T"])
        tmm(pP[0:1, 0:ES], AP(ones_s)[:, 0:1], AP(wkdw_s)[:, 0:ES],
            start=(kc == 0), stop=(kc == 7), need=["A"], inc=True)
    aact(AP(nrm_s), pP[0:1, 0:ES], AF.Ln, need=["T"])
    aact(AP(nrm_s), AP(nrm_s), AF.Exp, scale=-0.5)
    vop("tensor_scalar_min", AP(nrm_s), AP(nrm_s), 1e12, need=["A"])
    tmm(pP[:, 256:512], AP(ones_s)[0:1, :], AP(nrm_s), start=True, stop=True,
        need=["V"], inc=True)
    for kc in range(8):
        vop("tensor_mul", AP(mu_s)[:, kc, :], AP(mu_s)[:, kc, :], pP[:, 256:512],
            need=["T"])

    def phaseA(b, t, g):
        r0 = t * TC + g * 128
        dma("Di", AP(x_s), x_d[b, r0:r0 + 128, :], need=["V", "A"])
        st = AP(st_s)
        vop("reduce_sum", st[:, 0:1], AP(x_s), AX.X, need=[cur("Di")])
        aact(AP(wkdw_s), AP(x_s), AF.Square, need=[cur("Di"), "V", "T"])
        vop("reduce_sum", st[:, 1:2], AP(wkdw_s), AX.X, need=["A"])
        vop("tensor_scalar_mul", st[:, 2:3], st[:, 0:1], 1.0 / D)
        vop("tensor_scalar_mul", st[:, 3:4], st[:, 1:2], 1.0 / D, need=["A"])
        vop("tensor_mul", st[:, 8:9], st[:, 2:3], st[:, 2:3])
        vop("tensor_sub", st[:, 3:4], st[:, 3:4], st[:, 8:9])
        vop("tensor_scalar_add", st[:, 3:4], st[:, 3:4], 1e-5)
        aact(st[:, 9:10], st[:, 3:4], AF.Ln, need=["V"])
        aact(st[:, 4:5], st[:, 9:10], AF.Exp, scale=-0.5)
        vop("tensor_mul", st[:, 13:14], st[:, 2:3], st[:, 4:5], need=["A"])
        aact(st[:, 5:6], st[:, 4:5], AF.Identity, bias=1.0, need=["V"])
        aact(st[:, 6:7], st[:, 13:14], AF.Identity)
        aact(st[:, 7:8], st[:, 13:14], AF.Identity, scale=-1.0)
        vop("tensor_scalar", AP(s_s)[:, g, :], AP(x_s), st[:, 5:6], st[:, 6:7],
            ALU.mult, ALU.subtract, need=["A"])
        aact(AP(ln0tm_s), AP(x_s), AF.Identity, bias=st[:, 7:8], scale=st[:, 4:5],
             need=["V"])
        for j in range(8):
            ttr(pG[:, j, :], AP(ln0tm_s)[:, j * 128:(j + 1) * 128],
                need=(["A", "V"] if j == 0 else []), inc=(j == 7))
        vop("tensor_copy", AP(ln0fm_s)[:, :, 1 + g * 128: 1 + g * 128 + 128],
            pG[:, :, :], need=["T"])
        for kc in range(8):
            for n in range(2):
                tmm(pA[:, n * 512:(n + 1) * 512],
                    AP(ln0fm_s)[:, kc, 1 + g * 128: 1 + g * 128 + 128],
                    AP(wwT_s)[:, kc, n * 512:(n + 1) * 512],
                    start=(kc == 0), stop=(kc == 7),
                    need=(["V"] if kc == 0 else []), inc=(kc == 7 and n == 1))
        for kc in range(8):
            tmm(pP[:, 512:560], AP(ln0fm_s)[:, kc, 1 + g * 128: 1 + g * 128 + 128],
                AP(wgba_s)[:, kc, :], start=(kc == 0), stop=(kc == 7),
                need=(["V"] if kc == 0 else []), inc=(kc == 7))
        gsl = slice(g * 16, (g + 1) * 16)
        aact(AP(gate_s)[:, gsl], pP[:, 512:528], AF.Sigmoid, need=["T"])
        aact(AP(beta_s)[:, gsl], pP[:, 528:544], AF.Sigmoid)
        vop("tensor_add", AP(tmp16_s), pP[:, 544:560], AP(dtb_s), need=["T"])
        aact(AP(tmp16b_s), AP(tmp16_s), AF.Exp, need=["V"])
        aact(AP(tmp16_s), AP(tmp16b_s), AF.Ln, bias=1.0)
        vop("tensor_mul", AP(decay_s)[:, gsl], AP(tmp16_s), AP(negea_s), need=["A"])
        bAP = AP(beta_s)[:, gsl][:, :, None].broadcast_to([128, 16, 64])
        vop("tensor_mul", AP(vb_s)[:, g % 2, :].rearrange("p (h e) -> p h e", h=16),
            pA[:, :].rearrange("p (h e) -> p h e", h=16), bAP, need=["T", "A"])
        tmm(pP[:, 560:576], AP(trilI_s), AP(decay_s)[:, gsl], True, True,
            need=["V"], inc=False)
        tmm(pP[:, 576:592], AP(ones_s), AP(decay_s)[:, gsl], True, True)
        tmm(pP[:, 592:608], AP(utS_s), AP(decay_s)[:, gsl], True, True, inc=True)
        vop("tensor_copy", AP(dec_s)[:, gsl], pP[:, 560:576], need=["T"])
        aact(AP(dexp_s)[:, gsl], pP[:, 560:576], AF.Exp, need=["T"])
        aact(AP(lam_s)[:, gsl], pP[:, 576:592], AF.Exp)
        aact(AP(dw_s)[:, gsl], pP[:, 592:608], AF.Exp)
        vop("tensor_mul", AP(g2_s)[:, gsl], AP(gate_s)[:, gsl], AP(dexp_s)[:, gsl],
            need=["A"])

    def rep_phase(b, t, hh):
        rk = os.environ.get("KREP", "12345")
        csl = slice(hh * 32, hh * 32 + 32)
        if "1" in rk:
            ttr(pP[0:32, 608:736], AP(dec_s)[:, csl], need=["V", "A"], inc=True)
            aact(AP(decpm_s), pP[0:32, 608:736], AF.Identity, need=["T"])
        if "2" in rk:
            dma("Do", drow_d[hh], AP(decpm_s), need=["A"])
        if "3" in rk:
            dma("Do", AP(arena_s),
                drow_d[hh].broadcast_to([128, 32 * CH]),
                need=[cur("Do"), "V", "A"])
        if "4" in rk:
            a3 = AP(arena_s).rearrange("p (i c) -> p i c", i=32)
            mAP = AP(maskB_s)[:, None, :].broadcast_to([128, 32, 128])
            vop("tensor_add", a3, a3, mAP, need=[cur("Do")])
            dAP = AP(dec_s)[:, csl][:, :, None].broadcast_to([128, 32, 128])
            vop("tensor_sub", a3, a3, dAP)
        if "5" in rk:
            aact(AP(arena_s), AP(arena_s), AF.Exp, need=["V"])

    def scan_chunk(b, t, np_):
        sk = os.environ.get("KSCAN", "giaewkdvtfu")
        li = np_ % 2
        nsl = slice(np_ * 16, (np_ + 1) * 16)
        for half in range(2):
            if "g" not in sk:
                break
            hord = [half * 8 + q for q in (0, 2, 4, 6, 1, 3, 5, 7)]
            for idx, h in enumerate(hord):
                r = 64 * (h % 2)
                j = h // 2
                tmm(pG[:, h % 8, :],
                    AP(ln0fm_s)[r:r + 64, j, np_ * 128: np_ * 128 + 128],
                    AP(ln0fm_s)[r:r + 64, j, 1 + np_ * 128: 1 + np_ * 128 + 128],
                    True, True, need=(["V"] if idx == 0 else []),
                    inc=(idx == 7))
            lsl = AP(arena_s).rearrange("p (i c) -> p i c", i=32)[
                :, li * 16 + half * 8: li * 16 + half * 8 + 8, :]
            if "i" not in sk:
                continue
            vop("tensor_mul", AP(intra_s), pG[:, :, :], lsl, need=["T", "A"])
            if "a" not in sk:
                continue
            for h in range(half * 8, half * 8 + 8):
                tmm(pB[:, h * 64:(h + 1) * 64], AP(intra_s)[:, h % 8, :],
                    AP(vb_s)[:, li, h * 64:(h + 1) * 64], True, True,
                    need=(["V"] if h % 8 == 0 else []), inc=(h % 8 == 7))
        if "e" in sk:
            for idx, h in enumerate(list(range(0, 16, 2)) + list(range(1, 16, 2))):
                r = 64 * (h % 2)
                j = h // 2
                tmm(pA[:, h * 64:(h + 1) * 64],
                    AP(ln0fm_s)[r:r + 64, j, 1 + np_ * 128: 1 + np_ * 128 + 128],
                    AP(S_s)[r:r + 64, j, :], True, True,
                    need=(["V"] if idx == 0 else []), inc=(idx == 15))
        if "w" not in sk:
            return
        for j in range(8):
            ttr(pG[:, j, :], AP(ln0fm_s)[:, j, np_ * 128: np_ * 128 + 128],
                need=(["V"] if j == 0 else []), inc=(j == 7))
        dwAP = AP(dw_s)[:, nsl].rearrange("p (j par) -> p j par", par=2)[
            :, :, :, None].broadcast_to([128, 8, 2, 64])
        if "k" not in sk:
            return
        vop("tensor_mul",
            AP(wkdw_s).rearrange("p (j par dd) -> p j par dd", j=8, par=2),
            pG[:, :, :].rearrange("p j (par dd) -> p j par dd", par=2),
            dwAP, need=["T", "A"])
        if "d" not in sk:
            return
        for idx, h in enumerate(list(range(0, 16, 2)) + list(range(1, 16, 2))):
            r = 64 * (h % 2)
            j = h // 2
            tmm(pP[r:r + 64, j * 64:(j + 1) * 64],
                AP(wkdw_s)[:, j * 128 + (h % 2) * 64: j * 128 + (h % 2) * 64 + 64],
                AP(vb_s)[:, li, h * 64:(h + 1) * 64], True, True,
                need=(["V"] if idx == 0 else []), inc=(idx == 15))
        if "v" not in sk:
            return
        g2AP = AP(g2_s)[:, nsl][:, :, None].broadcast_to([128, 16, 64])
        gAP = AP(gate_s)[:, nsl][:, :, None].broadcast_to([128, 16, 64])
        o3v = lambda s: AP(s).rearrange("p (h e) -> p h e", h=16)
        vop("tensor_mul", o3v(x_s), pA[:, :].rearrange("p (h e) -> p h e", h=16),
            g2AP, need=["T"])
        vop("tensor_mul", o3v(ln0tm_s), pB[:, :].rearrange("p (h e) -> p h e", h=16),
            gAP)
        vop("tensor_add", AP(o3_s), AP(x_s), AP(ln0tm_s))
        if "t" not in sk:
            return
        for j in range(8):
            ttr(pG[:, j, :], AP(o3_s)[:, j * 128:(j + 1) * 128],
                need=(["V"] if j == 0 else []), inc=(j == 7))
        if "f" not in sk:
            return
        vop("tensor_copy", AP(ofm_s)[:, :, li * 128: li * 128 + 128], pG[:, :, :],
            need=["T"])
        if "u" not in sk:
            return
        for r in range(2):
            lAP = AP(lam_s)[64 * r:64 * r + 64,
                            np_ * 16 + r: np_ * 16 + 16: 2][:, :, None] \
                .broadcast_to([64, 8, 64])
            vop("tensor_mul", AP(S_s)[64 * r:64 * r + 64, :, :],
                AP(S_s)[64 * r:64 * r + 64, :, :], lAP,
                need=(["T"] if r == 0 else []))
        vop("tensor_add", AP(S_s)[:, :, :],
            AP(S_s)[:, :, :], pP[:, 0:512].rearrange("p (j e) -> p j e", j=8))

    def wout_half(b, t, hh):
        pmap = [pA, pB]
        tdone = [CNT["T"]] * 2
        for kc in range(8):
            dma("Di", AP(woT_b)[:, kc % 2, :], woT_d[kc * 128:(kc + 1) * 128, :],
                need=[("T", tdone[kc % 2])])
            for gl in range(2):
                for n in range(2):
                    tmm(pmap[gl][:, n * 512:(n + 1) * 512],
                        AP(ofm_s)[:, kc, gl * 128: gl * 128 + 128],
                        AP(woT_b)[:, kc % 2, n * 512:(n + 1) * 512],
                        start=(kc == 0), stop=(kc == 7),
                        need=[cur("Di")] + (["V"] if kc == 0 else []),
                        inc=(gl == 1 and n == 1))
            tdone[kc % 2] = CNT["T"]
        for gl in range(2):
            g = 2 * hh + gl
            vop("tensor_add", AP(s_s)[:, g, :], AP(s_s)[:, g, :], pmap[gl][:, :],
                need=["T"])
            r0 = t * TC + g * 128
            dma("Do", x1_d[b, r0:r0 + 128, :], AP(s_s)[:, g, :], need=["V"])

    def ffn_g(b, t, g):
        fk = os.environ.get("KFFN", "123456")
        x1 = AP(s_s)[:, g, :]
        st = AP(st_s)
        r0 = t * TC + g * 128

        def ln_stats(src, ci, need0):
            vop("reduce_sum", st[:, 0:1], src, AX.X, need=need0)
            aact(AP(wkdw_s), src, AF.Square, need=need0 + ["V"])
            vop("reduce_sum", st[:, 1:2], AP(wkdw_s), AX.X, need=["A"])
            vop("tensor_scalar_mul", st[:, 2:3], st[:, 0:1], 1.0 / D)
            vop("tensor_scalar_mul", st[:, 3:4], st[:, 1:2], 1.0 / D, need=["A"])
            vop("tensor_mul", st[:, 8:9], st[:, 2:3], st[:, 2:3])
            vop("tensor_sub", st[:, 3:4], st[:, 3:4], st[:, 8:9])
            vop("tensor_scalar_add", st[:, 3:4], st[:, 3:4], 1e-5)
            aact(st[:, 9:10], st[:, 3:4], AF.Ln, need=["V"])
            aact(st[:, 4:5], st[:, 9:10], AF.Exp, scale=-0.5)
            vop("tensor_mul", st[:, 13:14], st[:, 2:3], st[:, 4:5], need=["A"])
            aact(st[:, 7:8], st[:, 13:14], AF.Identity, scale=-1.0, need=["V"])

        if "1" not in fk:
            return
        ln_stats(x1, 0, ["V", "A", "T"])
        aact(AP(lnt_s), x1, AF.Identity, bias=st[:, 7:8], scale=st[:, 4:5],
             need=["V"])
        ln_stats(AP(lnt_s), 0, ["A"])
        aact(AP(o3_s), AP(lnt_s), AF.Identity, bias=st[:, 7:8], scale=st[:, 4:5],
             need=["V"])
        vop("tensor_mul", AP(xn_s), AP(o3_s), AP(gam_s), need=["A"])
        vop("tensor_add", AP(xn_s), AP(xn_s), AP(bet_s))
        aact(AP(wkdw_s), AP(xn_s), AF.Square, need=["V"])
        vop("reduce_sum", st[:, 1:2], AP(wkdw_s), AX.X, need=["A"])
        aact(st[:, 9:10], st[:, 1:2], AF.Ln, need=["V"])
        aact(st[:, 10:11], st[:, 9:10], AF.Exp, scale=-0.5)
        vop("tensor_scalar_min", st[:, 10:11], st[:, 10:11], 1e12, need=["A"])
        vop("tensor_mul", st[:, 11:12], st[:, 10:11], AP(scl_s))
        aact(AP(lnt_s), AP(xn_s), AF.Identity, bias=0.0, scale=st[:, 11:12],
             need=["V"])
        if "2" not in fk:
            return
        for j in range(8):
            ttr(pG[:, j, :], AP(lnt_s)[:, j * 128:(j + 1) * 128],
                need=(["A", "V"] if j == 0 else []), inc=(j == 7))
        vop("tensor_copy", AP(xufm_s)[:, :, :], pG[:, :, :], need=["T"])
        if "3" not in fk:
            return
        for esc in range(2):
            for kc in range(8):
                tmm(pP[:, esc * 128:(esc + 1) * 128],
                    AP(mu_s)[:, kc, esc * 128:(esc + 1) * 128],
                    AP(xufm_s)[:, kc, :], start=(kc == 0), stop=(kc == 7),
                    need=(["V"] if kc == 0 else []),
                    inc=(kc == 7 and esc == 1))
        vop("tensor_copy", AP(lg_s)[:, :, :],
            pP[:, 0:256].rearrange("p (i c) -> p i c", i=2), need=["T"])
        if "4" not in fk:
            return
        f4 = os.environ.get("KFF4", "abcdef")
        mz = AP(mz_s)
        bm = slice(b * 2, b * 2 + 2)
        if "a" in f4:
            for esc in range(2):
                vop("reduce_max", st[:, 12 + esc:13 + esc], AP(lg_s)[:, esc, :], AX.X)
        if "b" in f4:
            vop("tensor_max", st[:, 12:14], st[:, 12:14], mz[:, bm])
            vop("tensor_sub", st[:, 14:16], mz[:, bm], st[:, 12:14])
        if "c" in f4:
            aact(mz[:, 8 + b * 2: 8 + b * 2 + 2], st[:, 14:16], AF.Exp, need=["V"])
        if "d" in f4:
            vop("tensor_copy", mz[:, bm], st[:, 12:14])
            vop("tensor_scalar_mul", mz[:, 12 + b * 2: 12 + b * 2 + 2],
                st[:, 12:14], -1.0)
        if "e" in f4:
            for esc in range(2):
                aact(AP(pexp_s)[:, esc, :], AP(lg_s)[:, esc, :], AF.Exp,
                     bias=mz[:, 12 + b * 2 + esc: 12 + b * 2 + esc + 1],
                     need=["V"])
            for esc in range(2):
                vop("reduce_sum", st[:, 5 + esc: 6 + esc], AP(pexp_s)[:, esc, :],
                    AX.X, need=["A"])
        if "f" in f4:
            for esc in range(2):
                vop("scalar_tensor_tensor", mz[:, 4 + b * 2 + esc: 4 + b * 2 + esc + 1],
                    mz[:, 4 + b * 2 + esc: 4 + b * 2 + esc + 1],
                    mz[:, 8 + b * 2 + esc: 8 + b * 2 + esc + 1],
                    st[:, 5 + esc: 6 + esc], ALU.mult, ALU.add, need=["A"])
        if "5" not in fk:
            return
        for esc in range(2):
            ttr(pG[:, esc, :], AP(pexp_s)[:, esc, :],
                need=(["A", "V"] if esc == 0 else []))
            ttr(pG[:, 2 + esc, :], AP(lg_s)[:, esc, :], inc=(esc == 1))
        vop("tensor_copy", AP(pexptm_s).rearrange("p (i c) -> p i c", i=2),
            pG[:, 0:2, :], need=["T"])
        vop("tensor_copy", AP(lgtm_s).rearrange("p (i c) -> p i c", i=2),
            pG[:, 2:4, :])
        dma("Do", lg_d[b, r0:r0 + 128, :], AP(lgtm_s), need=["V"])
        if "6" not in fk:
            return
        for esc in range(2):
            for n in range(2):
                tmm((pA if esc == 0 else pB)[:, n * 512:(n + 1) * 512],
                    AP(pexptm_s)[:, esc * 128:(esc + 1) * 128],
                    AP(xn_s)[:, n * 512:(n + 1) * 512], True, True,
                    need=(["V"] if esc == 0 and n == 0 else []),
                    inc=(esc == 1 and n == 1))
        for esc in range(2):
            vop("scalar_tensor_tensor", AP(acc_s)[:, b * 2 + esc, :],
                AP(acc_s)[:, b * 2 + esc, :],
                mz[:, 8 + b * 2 + esc: 8 + b * 2 + esc + 1],
                (pA if esc == 0 else pB)[:, :], ALU.mult, ALU.add, need=["T"])

    def slot_phase():
        for b in range(2):
            for esc in range(2):
                sec = b * 2 + esc
                aact(AP(st_s)[:, 1:2], AP(mz_s)[:, 4 + sec: 5 + sec], AF.Ln,
                     need=["T", "V"])
                aact(AP(st_s)[:, 0:1], AP(st_s)[:, 1:2], AF.Exp, scale=-1.0)
                vop("tensor_scalar_mul", AP(acc_s)[:, sec, :],
                    AP(acc_s)[:, sec, :], AP(st_s)[:, 0:1], need=["T", "A"])
                for kc in range(8):
                    ttr(pG[:, kc, :], AP(acc_s)[:, sec, kc * 128:(kc + 1) * 128],
                        need=(["V"] if kc == 0 else []), inc=(kc == 7))
                oAP = bass.AP(sin_s, esc * 256 + b * 16,
                              [[4096, 128], [512, 8], [32, 8], [1, 16]])
                iAP = bass.AP(pG, 0, [[1024, 128], [128, 8], [16, 8], [1, 16]])
                vop("tensor_copy", oAP, iAP, need=["T"])
        dma("Di", AP(w1_s), w1_d[0], need=["T"])
        dma("Di", AP(b1_s), b1_d[0])
        dma("Di", AP(b2_s), b2_d[0])
        w1c, b1c, b2c, w2c = CNT["Di"], CNT["Di"], CNT["Di"], 0
        for e in range(NE):
            for kc in range(8):
                lh = bass.AP(sin_s, kc * 512 + (e // 8) * 256 + (e % 8) * 32,
                             [[4096, 128], [1, 32]])
                tmm(pP[0:32, 0:HF], lh, AP(w1_s)[:, kc, :],
                    start=(kc == 0), stop=(kc == 7),
                    need=[("Di", w1c)] + (["V"] if kc == 0 else []),
                    inc=(kc == 7))
            dma("Di", AP(w2_s), w2_d[e], need=["T"])
            w2c = CNT["Di"]
            vop("tensor_add", AP(hg_s), pP[0:32, 0:HF], AP(b1_s),
                need=["T", ("Di", b1c)])
            aact(AP(hg_s), AP(hg_s), AF.Gelu, need=["V"])
            for hc in range(2):
                ttr(pG[0:128, hc, 0:32], AP(hg_s)[:, hc * 128:(hc + 1) * 128],
                    need=(["A"] if hc == 0 else []), idn=AP(ident_s)[0:32, 0:32],
                    inc=(hc == 1))
            vop("tensor_copy", AP(hT_s).rearrange("p (hc q) -> p hc q", hc=2),
                pG[:, 0:2, 0:32], need=["T"])
            for hc in range(2):
                for n in range(2):
                    tmm(pA[0:32, n * 512:(n + 1) * 512], AP(hT_s)[:, hc * 32:hc * 32 + 32],
                        AP(w2_s)[:, hc, n * 512:(n + 1) * 512],
                        start=(hc == 0), stop=(hc == 1),
                        need=[("Di", w2c)] + (["V"] if hc == 0 and n == 0 else []),
                        inc=(hc == 1 and n == 1))
            if e + 1 < NE:
                dma("Di", AP(w1_s), w1_d[e + 1], need=["T"])
                w1c = CNT["Di"]
            vop("tensor_add", AP(wkdw_s)[0:32, :], pA[0:32, :], AP(b2_s),
                need=["T", "Do", ("Di", b2c)])
            if e + 1 < NE:
                dma("Di", AP(b1_s), b1_d[e + 1], need=["V"])
                b1c = CNT["Di"]
                dma("Di", AP(b2_s), b2_d[e + 1], need=["V"])
                b2c = CNT["Di"]
            for b in range(2):
                dma("Do", AP(so_s)[(e % 8) * 16:(e % 8) * 16 + 16, b * 2 + e // 8, :],
                    AP(wkdw_s)[b * 16: b * 16 + 16, :], need=["V"])

    def pass2_g(b, t, g):
        r0 = t * TC + g * 128
        dma("Di", AP(x_s), x1_d[b, r0:r0 + 128, :], need=["V", "Do"])
        dma("Di", AP(lgtm_s), lg_d[b, r0:r0 + 128, :], need=["V", "A", "T"])
        st = AP(st_s)
        vop("reduce_max", st[:, 0:1], AP(lgtm_s), AX.X, need=[cur("Di")])
        vop("tensor_scalar_mul", st[:, 1:2], st[:, 0:1], -1.0)
        aact(AP(pexptm_s), AP(lgtm_s), AF.Exp, bias=st[:, 1:2], need=["V"])
        vop("reduce_sum", st[:, 2:3], AP(pexptm_s), AX.X, need=["A"])
        aact(st[:, 8:9], st[:, 2:3], AF.Ln, need=["V"])
        aact(st[:, 3:4], st[:, 8:9], AF.Exp, scale=-1.0)
        for esc in range(2):
            ttr(pG[:, esc, :], AP(pexptm_s)[:, esc * 128:(esc + 1) * 128],
                need=(["A", "V"] if esc == 0 else []), inc=(esc == 1))
        vop("tensor_copy", AP(lg_s)[:, :, :], pG[:, 0:2, :], need=["T"])
        for esc in range(2):
            for n in range(2):
                tmm(pP[:, n * 512:(n + 1) * 512], AP(lg_s)[:, esc, :],
                    AP(so_s)[:, b * 2 + esc, n * 512:(n + 1) * 512],
                    start=(esc == 0), stop=(esc == 1),
                    need=(["V", "Do"] if esc == 0 and n == 0 else []),
                    inc=(esc == 1 and n == 1))
        vop("scalar_tensor_tensor", AP(xn_s), pP[:, :], st[:, 3:4], AP(x_s),
            ALU.mult, ALU.add, need=["T", "A"])
        dma("Do", out_d[b, r0:r0 + 128, :], AP(xn_s), need=["V"])

    _ks = os.environ.get("KSTAGES", "a,r,s,w,f,sl,p2").split(",")
    for b in range(BC):
        vop("memset", AP(S_s)[:, :, :], 0.0, need=["T"])
        vop("memset", AP(acc_s)[:, b * 2:b * 2 + 2, :], 0.0)
        vop("memset", AP(mz_s)[:, b * 2:b * 2 + 2], -100.0)
        vop("memset", AP(mz_s)[:, 4 + b * 2:4 + b * 2 + 2], 0.0)
        vop("memset", AP(ln0fm_s)[:, :, 0:1], 0.0)
        for t in range(NT):
            for hh in range(2):
                if "a" in _ks:
                    phaseA(b, t, 2 * hh)
                    phaseA(b, t, 2 * hh + 1)
                if "r" in _ks:
                    rep_phase(b, t, hh)
                if "s" in _ks:
                    scan_chunk(b, t, 2 * hh)
                    scan_chunk(b, t, 2 * hh + 1)
                if "w" in _ks:
                    wout_half(b, t, hh)
            if "f" in _ks:
                for g in range(4):
                    ffn_g(b, t, g)
            if "a" in _ks:
                vop("tensor_copy", AP(ln0fm_s)[:, :, 0:1],
                    AP(ln0fm_s)[:, :, 512:513], need=["T"])
    if "sl" in _ks:
        slot_phase()
    if "p2" in _ks:
        for b in range(BC):
            for t in range(NT):
                for g in range(4):
                    pass2_g(b, t, g)

    if os.environ.get("KDBG"):
        dma("Do", dbg_d[:, 0:4, :], AP(acc_s), need=["V", "A", "T"])
        if "sl" in _ks:
            dma("Do", dbg_d[:, 4:8, :], AP(so_s), need=["V", "A", "T"])
        dbg2_d = nc.dram_tensor("dbg2", [128, 16, 1024], F32, kind="ExternalOutput").ap()
        dbg3_d = nc.dram_tensor("dbg3", [128, 11, 1024], F32, kind="ExternalOutput").ap()
        dma("Do", dbg2_d[:, 0, 0:64], AP(dec_s))
        dma("Do", dbg2_d[:, 0, 64:128], AP(dexp_s))
        dma("Do", dbg2_d[:, 0, 128:192], AP(dw_s))
        dma("Do", dbg2_d[:, 0, 192:256], AP(lam_s))
        dma("Do", dbg2_d[:, 0, 256:320], AP(g2_s))
        dma("Do", dbg2_d[:, 0, 320:384], AP(gate_s))
        dma("Do", dbg2_d[:, 0, 384:448], AP(beta_s))
        if "r" in _ks:
            dma("Do", dbg2_d[:, 1:5, :], AP(arena_s).rearrange("p (a b) -> p a b", a=4))
        if "s" in _ks:
            dma("Do", dbg2_d[:, 5, :], AP(intra_s).rearrange("p a b -> p (a b)"))
        if "s" in _ks:
            dma("Do", dbg2_d[:, 6:8, :], AP(ofm_s).rearrange("p a b -> p (a b)").rearrange("p (a b) -> p a b", a=2))
            dma("Do", dbg2_d[:, 8, 0:512], AP(S_s).rearrange("p a b -> p (a b)"))
        dma("Do", dbg2_d[:, 9:11, :], AP(vb_s).rearrange("p a b -> p a b"))
        if "s" in _ks:
            dma("Do", dbg2_d[:, 11, :], AP(wkdw_s))
        dma("Do", dbg2_d[:, 12:16, :], AP(s_s))
        dma("Do", dbg3_d[:, 0, :], AP(x_s))
        dma("Do", dbg3_d[:, 1, :], AP(ln0tm_s))
        dma("Do", dbg3_d[:, 2:10, :], AP(ln0fm_s)[:, :, 0:513], allow_pad=True) if False else None
        dma("Do", dbg3_d[:, 2, 0:16], AP(st_s))
        dma("Do", dbg3_d[:, 3:11, :512], AP(ln0fm_s)[:, :, 1:513])
    emit("Do", None, need=["Do", "Di", "V", "A", "T"])

    def runner(h, items):
        for fn, waits, sem, amt in items:
            for s, c in waits:
                h.wait_ge(s, c)
            if fn is not None:
                ins = fn(h)
                if amt:
                    ins.then_inc(sem, amt)

    with nc.Block() as block:
        @block.tensor
        def _(h):
            runner(h, PROG["T"])

        @block.vector
        def _(h):
            runner(h, PROG["V"])

        @block.scalar
        def _(h):
            runner(h, PROG["A"])

        @block.sync
        def _(h):
            runner(h, PROG["Di"])

        @block.gpsimd
        def _(h):
            runner(h, PROG["Do"])

    es.close()
    return nc


def make_inputs(full, core):
    b0 = core * BC
    def kcmaj(w, nk):
        return np.ascontiguousarray(w.reshape(nk, 128, -1).transpose(1, 0, 2))

    wgba = np.concatenate([full["L0_attn_W_gate"].T, full["L0_attn_W_beta"].T,
                           full["L0_attn_W_alpha"].T], axis=1)
    return {
        "x": np.ascontiguousarray(full["L0_attn_out"][b0:b0 + BC]),
        "wwT": kcmaj(full["L0_attn_W_write"].T, 8),
        "wgbaT": kcmaj(wgba, 8),
        "woT": np.ascontiguousarray(full["L0_attn_W_out"].T),
        "alog_r": np.ascontiguousarray(np.tile(full["L0_attn_A_log"][None, :], (128, 1))),
        "dtb_r": np.ascontiguousarray(np.tile(full["L0_attn_dt_bias"][None, :], (128, 1))),
        "gam_r": np.ascontiguousarray(np.tile(full["L0_ffn_ln_gamma"][None, :], (128, 1))),
        "bet_r": np.ascontiguousarray(np.tile(full["L0_ffn_ln_beta"][None, :], (128, 1))),
        "mu2": kcmaj(full["L0_ffn_mu"].reshape(D, ES), 8),
        "scl_r": np.ascontiguousarray(np.tile(full["L0_ffn_scale"].reshape(1, 1), (128, 1))),
        "w1": np.ascontiguousarray(np.stack([kcmaj(full["L0_ffn_w1"][e], 8) for e in range(NE)])),
        "w2": np.ascontiguousarray(np.stack([kcmaj(full["L0_ffn_w2"][e], 2) for e in range(NE)])),
        "b1r": np.ascontiguousarray(np.tile(full["L0_ffn_b1"][:, None, :], (1, 32, 1))),
        "b2r": np.ascontiguousarray(np.tile(full["L0_ffn_b2"][:, None, :], (1, 32, 1))),
    }


def kernel(**inputs):
    T = inputs["L0_attn_out"].shape[1]
    nc = build(T)
    full = {k: np.asarray(v, dtype=np.float32) for k, v in inputs.items()}
    in_maps = [make_inputs(full, c) for c in range(8)]
    res = run_bass_kernel_spmd(nc, in_maps, core_ids=list(range(8)))
    out = np.concatenate([r["out"] for r in res.results], axis=0)
    return out.astype(np.float32)
